# version 02
# baseline (speedup 1.0000x reference)
import numpy as np
from contextlib import ExitStack

import concourse.bass as bass
import concourse.mybir as mybir
from concourse.bass_utils import run_bass_kernel_spmd

F32 = mybir.dt.float32
BF16 = mybir.dt.bfloat16
I16 = mybir.dt.int16
AF = mybir.ActivationFunctionType
ALU = mybir.AluOpType
AX = mybir.AxisListType

LN2 = float(np.log(2.0))
INV_LN2 = float(1.0 / np.log(2.0))

B, L, H = 4, 2048, 2048
E, N, TR, K = 4096, 16, 128, 4
EH = E // 2
I_FFN = 5632
NCORES = 8

GU_TILES = [6, 6, 6, 6, 5, 5, 5, 5]
GU_OFF = [0]
for t in GU_TILES:
    GU_OFF.append(GU_OFF[-1] + t * 128)
GU_PAD = 768


class Sem:
    def __init__(self, ctx, nc, name):
        self.h = ctx.enter_context(nc.semaphore(name))
        self.v = 0

    def inc(self, instr, k=1):
        instr.then_inc(self.h, k)
        self.v += k
        return instr

    def dma(self, instr):
        return self.inc(instr, 16)

    def wait(self, eng, v=None):
        eng.wait_ge(self.h, self.v if v is None else v)


_CHAIN_OPS = {
    "tensor_scalar", "tensor_tensor", "tensor_copy", "tensor_reduce",
    "reduce_max", "reduce_sum", "reciprocal", "memset", "memzero", "iota",
    "activation", "matmul", "transpose", "tensor_tensor_scan",
    "scalar_tensor_tensor", "select", "tensor_single_scalar",
    "tensor_scalar_add", "tensor_scalar_sub", "tensor_scalar_mul",
    "tensor_scalar_max", "tensor_scalar_min", "tensor_add", "tensor_sub",
    "tensor_mul", "tensor_max", "pool", "pool_avg", "pool_max", "square",
    "copy", "mul", "add", "sign", "sqrt", "partition_broadcast",
}


class Stream:

    def __init__(self, p, name):
        self.sem = p.sem(name)

    def dma(self, eng, dst, src, **kw):
        eng.dma_start(dst, src, **kw).then_inc(self.sem.h, 16)
        self.sem.v += 16
        return self.sem.v

    def wait(self, eng, v=None):
        self.sem.wait(eng, v)

    @property
    def v(self):
        return self.sem.v


class _Chained:

    def __init__(self, eng, sem):
        self._eng = eng
        self._sem = sem

    def __getattr__(self, name):
        f = getattr(self._eng, name)
        if name not in _CHAIN_OPS:
            return f

        def wrapped(*a, **k):
            if self._sem.v > 0:
                self._eng.wait_ge(self._sem.h, self._sem.v)
            ins = f(*a, **k)
            self._sem.inc(ins, 1)
            return ins
        return wrapped


class Prog:
    def __init__(self, name):
        self.nc = bass.Bass("TRN2", target_bir_lowering=False)
        self.ctx = ExitStack()
        self.name = name
        self._block = None
        self._chain_sems = {}

    def dram_in(self, name, shape, dtype):
        return self.nc.dram_tensor(name, list(shape), dtype, kind="ExternalInput")

    def dram_out(self, name, shape, dtype):
        return self.nc.dram_tensor(name, list(shape), dtype, kind="ExternalOutput")

    def sbuf(self, name, shape, dtype):
        return self.ctx.enter_context(self.nc.sbuf_tensor(name, list(shape), dtype))

    def psum(self, name, shape, dtype=F32):
        return self.ctx.enter_context(self.nc.psum_tensor(name, list(shape), dtype))

    def sem(self, name):
        return Sem(self.ctx, self.nc, name)

    def block(self):
        if self._block is None:
            self._block = self.ctx.enter_context(self.nc.Block())
        return self._block

    def eng_sem(self, engname):
        if engname not in self._chain_sems:
            self._chain_sems[engname] = Sem(self.ctx, self.nc,
                                            f"chain_{engname}")
        return self._chain_sems[engname]

    def on(self, engname, fn):
        if engname in ("sync", "tensor"):
            getattr(self.block(), engname)(fn)
            return
        sem = self.eng_sem(engname)
        getattr(self.block(), engname)(lambda e: fn(_Chained(e, sem)))

    def close(self):
        self.ctx.close()



def emit_consts(p, values):
    vals = sorted(set(float(v) for v in values))
    tile = p.sbuf("constpool", [128, max(1, len(vals))], F32)
    s_g = p.eng_sem("gpsimd")

    def g(eng):
        for j, v in enumerate(vals):
            eng.memset(tile[:, j:j + 1], v)
    for j, v in enumerate(vals):
        p.nc.const_aps.register(F32, v, tile[:, j:j + 1])
    p.on("gpsimd", g)
    at = s_g.v

    def s0(eng):
        s_g.wait(eng, at)
    p.on("scalar", s0)
    return at


def emit_ident(p):
    ident16 = p.sbuf("ident16", [128, 128], BF16)
    ident32 = p.sbuf("ident32", [128, 128], F32)
    pidx = p.sbuf("pidx", [128, 1], F32)
    s_g = p.eng_sem("gpsimd")
    s_v = p.eng_sem("vector")

    def g(eng):
        eng.iota(ident32[:, :], [[1, 128]], channel_multiplier=0,
                 allow_small_or_imprecise_dtypes=True)
        eng.iota(pidx[:, :], [[1, 1]], channel_multiplier=1,
                 allow_small_or_imprecise_dtypes=True)
    p.on("gpsimd", g)
    gv = s_g.v

    def v(eng):
        s_g.wait(eng, gv)
        eng.tensor_scalar(ident32[:, :], ident32[:, :], pidx[:, :1], None,
                          ALU.is_equal)
        eng.tensor_copy(ident16[:, :], ident32[:, :])
    p.on("vector", v)
    return ident16, ident32, s_v.v


def emit_fp8_tile(p, sv, ss, src, dst16, nrows, ncols, rowmax, lnw,
                  sa, sq, se_i16, qi_i16, rowscale_ap=None, pre_wait=None):
    r = slice(0, nrows)
    c = slice(0, ncols)

    def v1(e):
        ss.wait(e)
        if pre_wait is not None:
            pre_wait(e)
        e.tensor_reduce(rowmax[r, 0:1], src[r, c], AX.X, ALU.max,
                        apply_absolute_value=True)
        e.tensor_scalar(rowmax[r, 0:1], rowmax[r, 0:1], 1e-30, None, ALU.max)
        e.reciprocal(rowmax[r, 1:2], rowmax[r, 0:1])
        e.tensor_scalar(rowmax[r, 1:2], rowmax[r, 1:2], 448.0, None, ALU.mult)
        e.tensor_scalar(sa[r, c], src[r, c], rowmax[r, 1:2], None, ALU.mult)
        e.tensor_tensor(sq[r, c], sa[r, c], sa[r, c], ALU.mult)
        e.tensor_scalar(sq[r, c], sq[r, c], 2.0 ** -11.5, None, ALU.max)

    def s1(e):
        sv.wait(e)
        if rowscale_ap is None:
            e.activation(lnw[r, 0:1], rowmax[r, 0:1], AF.Ln, scale=1.0 / 448.0)
        else:
            e.activation(lnw[r, 0:1], rowscale_ap, AF.Ln)
        e.activation(lnw[r, 1:2], lnw[r, 0:1], AF.Copy, bias=-3.0 * LN2)
        e.activation(sq[r, c], sq[r, c], AF.Ln)

    def v2(e):
        ss.wait(e)
        e.tensor_scalar(se_i16[r, c], sq[r, c], 0.5 * INV_LN2, -0.5,
                        ALU.mult, ALU.add)

    def s2(e):
        sv.wait(e)
        e.activation(sq[r, c], se_i16[r, c], AF.Exp, scale=-LN2,
                     bias=3.0 * LN2)

    def v3(e):
        ss.wait(e)
        e.tensor_tensor(qi_i16[r, c], sa[r, c], sq[r, c], ALU.mult)

    def s3(e):
        sv.wait(e)
        e.activation(sq[r, c], se_i16[r, c], AF.Exp, scale=LN2,
                     bias=lnw[r, 1:2])

    def v4(e):
        ss.wait(e)
        e.tensor_tensor(dst16[r, c], qi_i16[r, c], sq[r, c], ALU.mult)

    p.on("vector", v1)
    src_free = sv.v
    p.on("scalar", s1)
    p.on("vector", v2)
    p.on("scalar", s2)
    p.on("vector", v3)
    p.on("scalar", s3)
    p.on("vector", v4)
    return {"src_free": src_free, "done": sv.v}



def build_l1():
    p = Prog("l1")
    nc = p.nc
    ct_s = p.dram_in("ct_s", [256, 4], F32)
    wmod_s = p.dram_in("wmod_s", [256, 12288], F32)
    w_in_s = p.dram_in("w_in_s", [2048, 4096], F32)
    wout_s = p.dram_in("wout_s", [2048, 2048], F32)
    wx_s = p.dram_in("wx_s", [2048, 160], F32)
    wdt_s = p.dram_in("wdt_s", [128, 2048], F32)
    ck_r = p.dram_in("ck_r", [128, 64], F32)
    wg_s = p.dram_in("wg_s", [GU_PAD, 2048], F32)
    wu_s = p.dram_in("wu_s", [GU_PAD, 2048], F32)
    wd_s = p.dram_in("wd_s", [256, 5632], F32)
    mod_part = p.dram_out("mod_part", [4, 12288], F32)
    w_in16 = p.dram_out("w_in16", [2048, 4096], BF16)
    wout16 = p.dram_out("wout16", [2048, 2048], BF16)
    wx16 = p.dram_out("wx16", [2048, 160], BF16)
    wdt16 = p.dram_out("wdt16", [128, 2048], BF16)
    diag_d = p.dram_out("diag_d", [16, 4, 128, 128], BF16)
    qg_sh = p.dram_out("qg_sh", [2048, GU_PAD], BF16)
    qu_sh = p.dram_out("qu_sh", [2048, GU_PAD], BF16)
    qd_sh = p.dram_out("qd_sh", [5632, 256], BF16)

    bufA = p.sbuf("bufA", [128, 6144], F32)
    bufB = p.sbuf("bufB", [128, 6144], BF16)
    bufC = p.sbuf("bufC", [128, 5632], F32)
    sa = p.sbuf("sa", [128, 5632], F32)
    sq = p.sbuf("sq", [128, 5632], F32)
    se = p.sbuf("se", [128, 5632], I16)
    qi = p.sbuf("qi", [128, 5632], I16)
    rowmax = p.sbuf("rowmax", [128, 2], F32)
    lnw = p.sbuf("lnw", [128, 2], F32)
    ctile = p.sbuf("ctile", [128, 8], F32)
    siluc = p.sbuf("siluc", [128, 8], BF16)
    modacc = p.sbuf("modacc", [4, 12288], F32)
    cksb = p.sbuf("cksb", [128, 64], F32)
    stage = p.sbuf("stage", [128, 256], BF16)
    q16 = p.sbuf("q16", [128, 5632], BF16)

    ps_mod = p.psum("ps_mod", [4, 512])
    ps_t0 = p.psum("ps_t0", [128, 128], BF16)
    ps_t1 = p.psum("ps_t1", [128, 128], BF16)

    s_v = p.eng_sem("vector")
    s_s = p.eng_sem("scalar")
    s_t = p.sem("s_t")
    ld_once = Stream(p, "ld_once")
    ld_a = Stream(p, "ld_a")
    ld_q = Stream(p, "ld_q")
    st_modp = Stream(p, "st_modp")
    st_cast = Stream(p, "st_cast")
    st_stage = [Stream(p, "st_stage0"), Stream(p, "st_stage1")]

    emit_consts(p, [1.0 / 448.0, -3.0 * LN2, 3.0 * LN2, -LN2, LN2])
    ident16, _, ident_at = emit_ident(p)

    def ld0(e):
        ld_once.dma(e, ctile[:, 0:4], ct_s[0:128, :])
        ld_once.dma(e, ctile[:, 4:8], ct_s[128:256, :])
        ld_once.dma(e, cksb[:, :], ck_r[:, :])
    p.on("sync", ld0)
    once_at = ld_once.v

    def sl0(e, w=once_at):
        ld_once.wait(e, w)
        e.activation(siluc[:, :], ctile[:, :], AF.Silu)
    p.on("scalar", sl0)

    def vz(e):
        e.memset(modacc[:, :], 0.0)
    p.on("vector", vz)

    for kt in range(2):
      for hf in range(2):
        def ldk(e, kt=kt, hf=hf, w=s_v.v):
            s_v.wait(e, w)
            ld_a.dma(e, bufA[:, :], wmod_s[kt * 128:(kt + 1) * 128,
                                           hf * 6144:(hf + 1) * 6144])
        p.on("sync", ldk)

        def vc(e, w=ld_a.v, wt=s_t.v):
            ld_a.wait(e, w)
            s_t.wait(e, wt)
            e.tensor_copy(bufB[:, :], bufA[:, :])
        p.on("vector", vc)

        for span in range(12):
            def tm(e, kt=kt, span=span):
                s_v.wait(e)
                s_s.wait(e)
                sl = slice(span * 512, (span + 1) * 512)
                s_t.inc(e.matmul(ps_mod[:, :], siluc[:, kt * 4:kt * 4 + 4],
                                 bufB[:, sl], start=True, stop=True))
            p.on("tensor", tm)

            def va(e, kt=kt, hf=hf, span=span):
                s_t.wait(e)
                sl = slice(hf * 6144 + span * 512, hf * 6144 + (span + 1) * 512)
                e.tensor_tensor(modacc[:, sl], modacc[:, sl],
                                ps_mod[:, :], ALU.add)
            p.on("vector", va)

    def st_mod(e, w=s_v.v):
        s_v.wait(e, w)
        st_modp.dma(e, mod_part[:, :], modacc[:, :])
    p.on("gpsimd", st_mod)

    cast_jobs = [(w_in_s, w_in16, 16, 4096), (wout_s, wout16, 16, 2048),
                 (wx_s, wx16, 16, 160), (wdt_s, wdt16, 1, 2048)]
    for src, dst, ntile, fdim in cast_jobs:
        for it in range(ntile):
            def ldc(e, src=src, it=it, fdim=fdim, w=s_v.v):
                s_v.wait(e, w)
                ld_a.dma(e, bufA[:, :fdim],
                         src[it * 128:(it + 1) * 128, :])
            p.on("sync", ldc)

            def vcc(e, it=it, fdim=fdim, w=ld_a.v, ws=st_cast.v, wt=s_t.v):
                ld_a.wait(e, w)
                st_cast.wait(e, ws)
                s_t.wait(e, wt)
                e.tensor_copy(bufB[:, :fdim], bufA[:, :fdim])
            p.on("vector", vcc)

            def stc(e, dst=dst, it=it, fdim=fdim, w=s_v.v):
                s_v.wait(e, w)
                st_cast.dma(e, dst[it * 128:(it + 1) * 128, :],
                            bufB[:, :fdim])
            p.on("gpsimd", stc)

    for ec in range(16):
        for tap in range(4):
            j = ec * 4 + tap
            half = j % 2

            def vd(e, j=j, half=half, ws=st_stage[half].v, wo=once_at):
                st_stage[half].wait(e, ws)
                ld_once.wait(e, wo)
                e.tensor_scalar(
                    stage[:, half * 128:(half + 1) * 128], ident16[:, :],
                    cksb[:, j:j + 1], None, ALU.mult)
            p.on("vector", vd)

            def std(e, ec=ec, tap=tap, half=half, w=s_v.v):
                s_v.wait(e, w)
                st_stage[half].dma(e, diag_d[ec, tap, :, :],
                                   stage[:, half * 128:(half + 1) * 128])
            p.on("gpsimd", std)

    my_tiles = 6
    quant_jobs = [(wg_s, qg_sh, my_tiles, 2048), (wu_s, qu_sh, my_tiles, 2048),
                  (wd_s, qd_sh, 2, 5632)]
    prev_v1 = s_v.v
    for src, dst, ntile, ncols in quant_jobs:
        for it in range(ntile):
            bufc_free_at = prev_v1

            def ldq(e, src=src, it=it, ncols=ncols, w=bufc_free_at):
                s_v.wait(e, w)
                ld_q.dma(e, bufC[:, :ncols],
                         src[it * 128:(it + 1) * 128, :])
            p.on("sync", ldq)
            ld_at = ld_q.v

            qm = emit_fp8_tile(p, s_v, s_s, bufC, q16, 128, ncols, rowmax, lnw,
                               sa, sq, se, qi,
                               pre_wait=lambda e, w=ld_at: ld_q.wait(e, w))
            prev_v1 = qm["src_free"]
            q16_at = qm["done"]

            nblk = ncols // 128
            evac_at = {}
            for cb in range(nblk):
                pst = ps_t0 if cb % 2 == 0 else ps_t1
                half = cb % 2

                def tt(e, cb=cb, pst=pst, q=q16_at, ev=evac_at.get(cb - 2),
                       iw=ident_at):
                    s_v.wait(e, max(q, iw))
                    if ev is not None:
                        s_v.wait(e, ev)
                    e.transpose(pst[:, :], q16[:, cb * 128:(cb + 1) * 128],
                                ident16[:, :]).then_inc(s_t.h, 1)
                s_t.v += 1
                p.on("tensor", tt)

                def tv(e, pst=pst, half=half, t_at=s_t.v,
                       g=st_stage[half].v):
                    s_t.wait(e, t_at)
                    st_stage[half].wait(e, g)
                    e.tensor_copy(
                        stage[:, half * 128:(half + 1) * 128], pst[:, :])
                p.on("vector", tv)
                evac_at[cb] = s_v.v

                def ts(e, dst=dst, it=it, cb=cb, half=half, v_at=s_v.v):
                    s_v.wait(e, v_at)
                    st_stage[half].dma(
                        e, dst[cb * 128:(cb + 1) * 128,
                               it * 128:(it + 1) * 128],
                        stage[:, half * 128:(half + 1) * 128])
                p.on("gpsimd", ts)

    def fin(e):
        for st in (st_modp, st_cast, st_stage[0], st_stage[1]):
            st.wait(e)
    p.on("gpsimd", fin)
    return p



def build_l2():
    p = Prog("l2")
    x_b = p.dram_in("x_b", [2048, 2048], F32)
    mod_all = p.dram_in("mod_all", [8, 4, 12288], F32)
    w_in16 = p.dram_in("w_in16", [2048, 4096], BF16)
    diag_d = p.dram_in("diag_d", [16, 4, 128, 128], BF16)
    cb_r = p.dram_in("cb_r", [128, 16], F32)
    wx16 = p.dram_in("wx16", [2048, 160], BF16)

    mod_flat = p.dram_out("mod_flat", [4, 12288], F32)
    hidden_d = p.dram_out("hidden_d", [2048, 2048], BF16)
    gs_d = p.dram_out("gs_d", [2048, 2048], BF16)
    ssmp_part = p.dram_out("ssmp_part", [2048, 160], F32)

    msum = p.sbuf("msum", [128, 384], F32)
    mtile = p.sbuf("mtile", [128, 2 * 384], F32)
    saca = p.sbuf("saca", [128, 32], F32)
    xbuf = p.sbuf("xbuf", [128, 2 * 2048], F32)
    sums = p.sbuf("sums", [128, 8], F32)
    stat = p.sbuf("stat", [128, 8], F32)
    dump16 = p.sbuf("dump16", [128, 1024], BF16)
    z16 = p.sbuf("z16", [128, 4 * 2048], BF16)
    hT = p.sbuf("hT", [128, 16 * 2048], BF16)
    wbuf = p.sbuf("wbuf", [128, 2 * 8192], BF16)
    hpre = p.sbuf("hpre", [128, 2 * 2051], BF16)
    hid2 = p.sbuf("hid2", [128, 2 * 2048], BF16)
    gs_sb = p.sbuf("gs_sb", [128, 2 * 2048], BF16)
    dbuf = p.sbuf("dbuf", [128, 2 * 512], BF16)
    wxs = p.sbuf("wxs", [128, 16 * 160], BF16)
    cbs = p.sbuf("cbs", [128, 16], F32)
    ssm_sb = p.sbuf("ssm_sb", [128, 2560], F32)

    s_ld = p.sem("s_ld")
    s_v = p.eng_sem("vector")
    s_s = p.eng_sem("scalar")
    s_t = p.sem("s_t")
    ld_m = [Stream(p, "ld_m0"), Stream(p, "ld_m1")]
    ld_x = [Stream(p, "ld_x0"), Stream(p, "ld_x1")]
    ld_w = [Stream(p, "ld_w0"), Stream(p, "ld_w1")]
    ld_d = [Stream(p, "ld_d0"), Stream(p, "ld_d1")]
    st_m = Stream(p, "st_m")
    st_g = [Stream(p, "st_g0"), Stream(p, "st_g1")]
    st_hd = [Stream(p, "st_h0"), Stream(p, "st_h1")]
    st_x = Stream(p, "st_x")

    emit_consts(p, [1.0 / 2048.0, -0.5, 1.0])
    ident16, _, ident_at = emit_ident(p)

    ln_psum_scope = ExitStack()
    ps_z = ln_psum_scope.enter_context(
        p.nc.psum_tensor("ps_z", [128, 512], BF16))

    def vz_mod(e):
        e.memset(msum[:, :], 0.0)
    p.on("vector", vz_mod)
    for j in range(8):
        half = j % 2

        def ldj(e, j=j, half=half, w=max(0, s_v.v - 1)):
            s_v.wait(e, w)
            srcv = mod_all[j, :, :].rearrange("a f -> (a f)") \
                .rearrange("(p f) -> p f", p=128)
            ld_m[half].dma(e, mtile[:, half * 384:half * 384 + 384], srcv)
        p.on("sync", ldj)

        def vadd(e, half=half, w=ld_m[half].v):
            ld_m[half].wait(e, w)
            e.tensor_tensor(msum[:, :], msum[:, :],
                            mtile[:, half * 384:half * 384 + 384], ALU.add)
        p.on("vector", vadd)
    msum_at = s_v.v

    def st_mod(e, w=msum_at):
        s_v.wait(e, w)
        st_m.dma(e, mod_flat[:, :].rearrange("a f -> (a f)")
                 .rearrange("(p f) -> p f", p=128), msum[:, :])
    p.on("gpsimd", st_mod)
    modflat_at = st_m.v

    def ld_saca(e, w=modflat_at):
        st_m.wait(e, w)
        e.dma_start(saca[:, 0:16],
                    mod_flat[0, 0:2048].rearrange("(hk p) -> p hk", p=128),
                    allow_slow_non_contiguous=True).then_inc(s_ld.h, 16)
        e.dma_start(saca[:, 16:32],
                    mod_flat[0, 2048:4096].rearrange("(hk p) -> p hk", p=128),
                    allow_slow_non_contiguous=True).then_inc(s_ld.h, 16)
        e.dma_start(cbs[:, :], cb_r[:, :]).then_inc(s_ld.h, 16)
        e.dma_start(wxs[:, :].rearrange("p (ec d) -> p ec d", d=160),
                    wx16[:, :].rearrange("(ec p) d -> p ec d", p=128)) \
            .then_inc(s_ld.h, 16)
    p.on("sync", ld_saca)
    s_ld.v += 64
    saca_ld = s_ld.v

    def v_ca1(e, w=saca_ld):
        s_ld.wait(e, w)
        e.tensor_scalar(saca[:, 16:32], saca[:, 16:32], 1.0, None, ALU.add)
    p.on("vector", v_ca1)
    saca_at = s_v.v

    z_done = {}
    evz_at = 0
    for tt in range(16):
        xslot = tt % 2

        def ldx(e, tt=tt, xslot=xslot, w=z_done.get(tt - 2, 0)):
            s_v.wait(e, w)
            ld_x[xslot].dma(e, xbuf[:, xslot * 2048:(xslot + 1) * 2048],
                            x_b[tt * 128:(tt + 1) * 128, :])
        p.on("sync", ldx)
        xld = ld_x[xslot].v

        def s_stats(e, xslot=xslot, w=xld):
            ld_x[xslot].wait(e, w)
            x0 = xbuf[:, xslot * 2048:xslot * 2048 + 1024]
            x1 = xbuf[:, xslot * 2048 + 1024:xslot * 2048 + 2048]
            e.activation(dump16[:, :], x0, AF.Copy, accum_out=sums[:, 0:1])
            e.activation(dump16[:, :], x1, AF.Copy, accum_out=sums[:, 1:2])
            e.activation(dump16[:, :], x0, AF.Square, accum_out=sums[:, 2:3])
            e.activation(dump16[:, :], x1, AF.Square, accum_out=sums[:, 3:4])
        p.on("scalar", s_stats)
        stats_at = s_s.v

        def v_stat(e, w=stats_at):
            s_s.wait(e, w)
            e.tensor_tensor(sums[:, 4:5], sums[:, 0:1], sums[:, 1:2], ALU.add)
            e.tensor_scalar(stat[:, 0:1], sums[:, 4:5], 1.0 / 2048.0,
                            None, ALU.mult)
            e.tensor_tensor(sums[:, 5:6], sums[:, 2:3], sums[:, 3:4], ALU.add)
            e.tensor_scalar(sums[:, 6:7], sums[:, 5:6], 1.0 / 2048.0,
                            None, ALU.mult)
            e.tensor_tensor(sums[:, 7:8], stat[:, 0:1], stat[:, 0:1],
                            ALU.mult)
            e.tensor_tensor(sums[:, 7:8], sums[:, 6:7], sums[:, 7:8],
                            ALU.subtract)
            e.tensor_scalar(sums[:, 7:8], sums[:, 7:8], 1e-5, None, ALU.add)
        p.on("vector", v_stat)
        veps_at = s_v.v

        def s_rs(e, w=veps_at):
            s_v.wait(e, w)
            e.activation(stat[:, 1:2], sums[:, 7:8], AF.Ln)
            e.activation(stat[:, 2:3], stat[:, 1:2], AF.Exp, scale=-0.5)
        p.on("scalar", s_rs)
        rs_at = s_s.v

        def v_z(e, tt=tt, xslot=xslot, w=rs_at):
            s_s.wait(e, w)
            s_t.wait(e)
            zsl = z16[:, (tt % 4) * 2048:(tt % 4 + 1) * 2048]
            e.tensor_scalar(zsl, xbuf[:, xslot * 2048:(xslot + 1) * 2048],
                            stat[:, 0:1], stat[:, 2:3],
                            ALU.subtract, ALU.mult)
        p.on("vector", v_z)
        z_done[tt] = s_v.v

        if tt % 4 == 3:
            ttg = tt // 4
            for hk in range(16):
                def t_tr(e, hk=hk, w=max(z_done[tt], ident_at), ev=evz_at):
                    s_v.wait(e, w)
                    s_s.wait(e, ev)
                    for q in range(4):
                        ins = e.transpose(
                            ps_z[:, q * 128:(q + 1) * 128],
                            z16[:, q * 2048 + hk * 128:
                                q * 2048 + (hk + 1) * 128],
                            ident16[:, :])
                    s_t.inc(ins)
                p.on("tensor", t_tr)
                tr_at = s_t.v

                def s_ev(e, ttg=ttg, hk=hk, w=tr_at, w2=saca_at):
                    s_t.wait(e, w)
                    s_v.wait(e, w2)
                    e.activation(
                        hT[:, hk * 2048 + ttg * 512:hk * 2048 +
                           (ttg + 1) * 512],
                        ps_z[:, :], AF.Identity,
                        scale=saca[:, 16 + hk:17 + hk],
                        bias=saca[:, hk:hk + 1])
                p.on("scalar", s_ev)
                evz_at = s_s.v
    hT_at = s_s.v

    ln_psum_scope.close()
    ps_ip = p.psum("ps_ip", [128, 512])
    ps_cv = p.psum("ps_cv", [128, 512])
    ps_wx = p.psum("ps_wx", [128, 3072])

    def wx_slot(ts8):
        return (ts8 // 3) * 512 + (ts8 % 3) * 160

    ip_mm = {}
    ip_ev = {}
    cv_mm = {}
    cv_ev = {}
    gs_st = {}
    wx_mm = {}
    halo_at = {}

    for grp in range(8):
        wslot = grp % 2
        wfree = ip_mm.get(grp * 4 - 5, 0)

        def ldw(e, grp=grp, wslot=wslot, w=wfree):
            s_t.wait(e, w)
            ld_w[wslot].dma(
                e,
                wbuf[:, wslot * 8192:(wslot + 1) * 8192]
                .rearrange("p (hk e) -> p hk e", e=512),
                w_in16[:, grp * 512:(grp + 1) * 512]
                .rearrange("(hk p) e -> p hk e", p=128))
        p.on("sync", ldw)
        wld = ld_w[wslot].v

        for q in range(4):
            ecb = grp * 4 + q
            for tspan in range(4):
                pb = tspan % 2
                prev_ev = ip_ev.get(ecb - (1 if tspan >= 2 else 1), 0)
                guard_ev = s_s.v

                def tmm_grpwait(e, w1=wld, w2=hT_at, w3=guard_ev,
                                wslot=wslot):
                    ld_w[wslot].wait(e, w1)
                    s_s.wait(e, max(w2, w3))
                for hk in range(16):
                    def tmm(e, wslot=wslot, q=q, hk=hk, tspan=tspan, pb=pb,
                            first=(hk == 0), gw=tmm_grpwait):
                        if first:
                            gw(e)
                        ins = e.matmul(
                            ps_ip[:, :],
                            wbuf[:, wslot * 8192 + hk * 512 + q * 128:
                                 wslot * 8192 + hk * 512 + (q + 1) * 128],
                            hT[:, hk * 2048 + tspan * 512:
                               hk * 2048 + (tspan + 1) * 512],
                            start=(hk == 0), stop=(hk == 15),
                            skip_group_check=True)
                        if hk == 15:
                            ins.then_inc(s_t.h, 1)
                    p.on("tensor", tmm)
                s_t.v += 1
                mm_at = s_t.v

                def s_ipev(e, ecb=ecb, tspan=tspan, pb=pb, w=mm_at,
                           wst=gs_st.get(ecb - 2, 0)):
                    s_t.wait(e, w)
                    if ecb >= 16:
                        st_g[ecb % 2].wait(e, wst)
                    psl = ps_ip[:, :]
                    if ecb < 16:
                        sl = slice((ecb % 2) * 2051 + 3 + tspan * 512,
                                   (ecb % 2) * 2051 + 3 + (tspan + 1) * 512)
                        e.activation(hpre[:, sl], psl, AF.Copy)
                    else:
                        sl = slice((ecb % 2) * 2048 + tspan * 512,
                                   (ecb % 2) * 2048 + (tspan + 1) * 512)
                        e.activation(gs_sb[:, sl], psl, AF.Silu)
                p.on("scalar", s_ipev)
            ip_mm[ecb] = s_t.v
            ip_ev[ecb] = s_s.v

            if ecb >= 16:
                def st_gs(e, ec=ecb - 16, slot=ecb % 2, w=ip_ev[ecb]):
                    s_s.wait(e, w)
                    st_g[slot].dma(e, gs_d[ec * 128:(ec + 1) * 128, :],
                                   gs_sb[:, slot * 2048:(slot + 1) * 2048])
                p.on("gpsimd", st_gs)
                gs_st[ecb] = st_g[ecb % 2].v
                continue

            ec = ecb
            slot = ec % 2

            def v_halo(e, ec=ec, slot=slot, w=cv_mm.get(ec - 2, 0)):
                s_t.wait(e, w)
                e.memset(hpre[:, slot * 2051:slot * 2051 + 3], 0.0)
            p.on("vector", v_halo)
            halo_at[ec] = s_v.v

            def ldd(e, ec=ec, slot=slot, w=cv_mm.get(ec - 2, 0)):
                s_t.wait(e, w)
                ld_d[slot].dma(e, dbuf[:, slot * 512:(slot + 1) * 512]
                               .rearrange("p (k f) -> p k f", f=128),
                               diag_d[ec, :, :, :].rearrange("k p f -> p k f"))
            p.on("sync", ldd)
            dld = ld_d[slot].v

            for tspan in range(4):
                cv_guard = s_s.v
                for tap in range(4):
                    def tcv(e, ec=ec, slot=slot, tap=tap, tspan=tspan,
                            wd=dld, wip=ip_ev[ec], wz=halo_at[ec],
                            wev=cv_guard, first=(tap == 0)):
                        if first:
                            ld_d[slot].wait(e, wd)
                            s_s.wait(e, max(wip, wev))
                            s_v.wait(e, wz)
                        sl = slice(slot * 2051 + tspan * 512 + tap,
                                   slot * 2051 + tspan * 512 + tap + 512)
                        ins = e.matmul(
                            ps_cv[:, :],
                            dbuf[:, slot * 512 + tap * 128:
                                 slot * 512 + (tap + 1) * 128],
                            hpre[:, sl],
                            start=(tap == 0), stop=(tap == 3),
                            skip_group_check=True)
                        if tap == 3:
                            ins.then_inc(s_t.h, 1)
                    p.on("tensor", tcv)
                s_t.v += 1
                cvm = s_t.v

                def s_cvev(e, ec=ec, slot=slot, tspan=tspan, w=cvm,
                           wc=saca_ld, wwx=wx_mm.get(ec - 2, 0),
                           wsh=st_hd[ec % 2].v):
                    s_t.wait(e, w)
                    s_ld.wait(e, wc)
                    s_t.wait(e, max(w, wwx))
                    st_hd[ec % 2].wait(e, wsh)
                    e.activation(
                        hid2[:, slot * 2048 + tspan * 512:
                             slot * 2048 + (tspan + 1) * 512],
                        ps_cv[:, :], AF.Silu, bias=cbs[:, ec:ec + 1])
                p.on("scalar", s_cvev)
            cv_mm[ec] = s_t.v
            cv_ev[ec] = s_s.v

            def st_h(e, ec=ec, slot=slot, w=cv_ev[ec]):
                s_s.wait(e, w)
                st_hd[slot].dma(e, hidden_d[ec * 128:(ec + 1) * 128, :],
                                hid2[:, slot * 2048:(slot + 1) * 2048])
            p.on("gpsimd", st_h)

            for ts8 in range(16):
                def twx(e, ec=ec, slot=slot, ts8=ts8, wh=cv_ev[ec],
                        wl=saca_ld, first=(ts8 == 0)):
                    if first:
                        s_s.wait(e, wh)
                        s_ld.wait(e, wl)
                    ins = e.matmul(
                        ps_wx[:, wx_slot(ts8):wx_slot(ts8) + 160],
                        hid2[:, slot * 2048 + ts8 * 128:
                             slot * 2048 + (ts8 + 1) * 128],
                        wxs[:, ec * 160:(ec + 1) * 160],
                        start=(ec == 0 and ts8 % 3 == 0), stop=(ec == 15),
                        skip_group_check=True)
                    if ts8 == 15:
                        ins.then_inc(s_t.h, 1)
                p.on("tensor", twx)
            s_t.v += 1
            wx_mm[ec] = s_t.v

    def v_wx(e, w=wx_mm[15]):
        s_t.wait(e, w)
        for ts8 in range(16):
            e.tensor_copy(ssm_sb[:, ts8 * 160:(ts8 + 1) * 160],
                          ps_wx[:, wx_slot(ts8):wx_slot(ts8) + 160])
    p.on("vector", v_wx)
    wxv = s_v.v

    def st_wx(e, w=wxv):
        s_v.wait(e, w)
        st_x.dma(e, ssmp_part[:, :].rearrange("(s p) d -> p s d", p=128),
                 ssm_sb[:, :].rearrange("p (s d) -> p s d", d=160))
    p.on("gpsimd", st_wx)

    def fin(e):
        for st in (st_m, st_g[0], st_g[1], st_hd[0], st_hd[1], st_x):
            st.wait(e)
    p.on("gpsimd", fin)
    return p



def build_l3():
    p = Prog("l3")
    ssmp_a = p.dram_in("ssmp_a", [2048, 160], F32)
    ssmp_b = p.dram_in("ssmp_b", [2048, 160], F32)
    wdt16 = p.dram_in("wdt16", [128, 2048], BF16)
    bdt_r = p.dram_in("bdt_r", [128, 16], F32)
    alog_r = p.dram_in("alog_r", [128, 256], F32)
    d_r = p.dram_in("d_r", [128, 16], F32)
    hidden_d = p.dram_in("hidden_d", [2048, 2048], BF16)
    gs_d = p.dram_in("gs_d", [2048, 2048], BF16)

    y_d = p.dram_out("y_d", [2048, 2048], BF16)

    ssm_sb = p.sbuf("ssm_sb", [128, 2560], F32)
    ssm_sb2 = p.sbuf("ssm_sb2", [128, 2560], F32)
    wdts = p.sbuf("wdts", [128, 2048], BF16)
    bdts = p.sbuf("bdts", [128, 16], F32)
    aneg = p.sbuf("aneg", [128, 256], F32)
    ds = p.sbuf("ds", [128, 16], F32)
    dtT = p.sbuf("dtT", [128, 2048], BF16)
    bcT = p.sbuf("bcT", [32, 2048], BF16)
    sel32 = p.sbuf("sel32", [32, 32 * 128], BF16)
    sftp = p.sbuf("sftp", [128, 2048], F32)
    dts4 = p.sbuf("dts4", [128, 4 * 2048], BF16)
    du4 = p.sbuf("du4", [128, 4 * 2048], BF16)
    hid4 = p.sbuf("hid4", [128, 4 * 2048], BF16)
    gs4 = p.sbuf("gs4", [128, 4 * 2048], BF16)
    yacc4 = p.sbuf("yacc4", [128, 4 * 2048], F32)
    dab = p.sbuf("dab", [128, 2 * 2048], BF16)
    dbub = p.sbuf("dbub", [128, 2 * 2048], BF16)
    states = p.sbuf("states", [128, 2048], BF16)
    cs = p.sbuf("cs", [128, 2048], BF16)
    bbc = p.sbuf("bbc", [128, 2048], BF16)
    cbc = p.sbuf("cbc", [128, 2048], BF16)
    ytmp = p.sbuf("ytmp", [128, 2048], F32)
    y16 = p.sbuf("y16", [128, 2048], BF16)

    ps_a = p.psum("ps_a", [128, 512])
    ps_bc = p.psum("ps_bc", [128, 512])

    s_v = p.eng_sem("vector")
    s_s = p.eng_sem("scalar")
    s_g = p.eng_sem("gpsimd")
    s_t = p.sem("s_t")
    ld_once = Stream(p, "ld_once")
    ld_hg = Stream(p, "ld_hg")
    st_y = Stream(p, "st_y")

    emit_consts(p, [-3.0 * LN2, 3.0 * LN2, -LN2, LN2, 1.0, -1.0])
    ident16, ident32, ident_at = emit_ident(p)

    selw = p.sbuf("selw", [32, 4096], F32)
    selp = p.sbuf("selp", [32, 1], F32)

    def g_ones(e):
        e.iota(selw[:, :].rearrange("p (a b) -> p a b", b=128),
               [[1, 32], [0, 128]], channel_multiplier=0,
               allow_small_or_imprecise_dtypes=True)
        e.iota(selp[:, :], [[1, 1]], channel_multiplier=1,
               allow_small_or_imprecise_dtypes=True)
    p.on("gpsimd", g_ones)
    gsel_at = s_g.v

    def v_sel(e, w=gsel_at):
        s_g.wait(e, w)
        e.tensor_scalar(selw[:, :], selw[:, :], selp[:, 0:1], None,
                        ALU.is_equal)
        e.tensor_copy(sel32[:, :], selw[:, :])
    p.on("vector", v_sel)
    ones_at = s_g.v

    def ld0(e):
        ld_once.dma(e, ssm_sb[:, :].rearrange("p (s d) -> p s d", d=160),
                    ssmp_a[:, :].rearrange("(s p) d -> p s d", p=128))
        ld_once.dma(e, ssm_sb2[:, :].rearrange("p (s d) -> p s d", d=160),
                    ssmp_b[:, :].rearrange("(s p) d -> p s d", p=128))
        ld_once.dma(e, wdts[:, :], wdt16[:, :])
        ld_once.dma(e, bdts[:, :], bdt_r[:, :])
        ld_once.dma(e, aneg[:, :], alog_r[:, :])
        ld_once.dma(e, ds[:, :], d_r[:, :])
    p.on("sync", ld0)
    ld0_at = ld_once.v

    def v_add(e, w=ld0_at):
        ld_once.wait(e, w)
        e.tensor_tensor(ssm_sb[:, :], ssm_sb[:, :], ssm_sb2[:, :], ALU.add)
    p.on("vector", v_add)
    ssm_at = s_v.v

    def s_a(e, w=ld0_at):
        ld_once.wait(e, w)
        e.activation(aneg[:, :], aneg[:, :], AF.Exp)
    p.on("scalar", s_a)
    sa_at = s_s.v

    def v_an(e, w=sa_at):
        s_s.wait(e, w)
        e.tensor_scalar(aneg[:, :], aneg[:, :], -1.0, None, ALU.mult)
    p.on("vector", v_an)
    aneg_at = s_v.v

    evac_at = 0
    for ts8 in range(16):
        def t_dt(e, ts8=ts8, w=ssm_at, ev=evac_at, iw=ident_at):
            s_v.wait(e, max(w, iw))
            s_s.wait(e, ev)
            e.transpose(ps_a[:, 0:128],
                        ssm_sb[:, ts8 * 160:ts8 * 160 + 128],
                        ident32[:, :]).then_inc(s_t.h, 1)
        s_t.v += 1
        p.on("tensor", t_dt)

        def s_dt(e, ts8=ts8, w=s_t.v):
            s_t.wait(e, w)
            e.activation(dtT[:, ts8 * 128:(ts8 + 1) * 128], ps_a[:, 0:128],
                         AF.Copy)
        p.on("scalar", s_dt)
        evac_at = s_s.v
    for ts8 in range(16):
        def t_bc(e, ts8=ts8, w=ssm_at, ev=evac_at, iw=ident_at):
            s_v.wait(e, max(w, iw))
            s_s.wait(e, ev)
            e.transpose(ps_bc[0:32, 0:128],
                        ssm_sb[:, ts8 * 160 + 128:ts8 * 160 + 160],
                        ident32[:, :]).then_inc(s_t.h, 1)
        s_t.v += 1
        p.on("tensor", t_bc)

        def s_bc(e, ts8=ts8, w=s_t.v):
            s_t.wait(e, w)
            e.activation(bcT[:, ts8 * 128:(ts8 + 1) * 128], ps_bc[0:32, 0:128],
                         AF.Copy)
        p.on("scalar", s_bc)
        evac_at = s_s.v
    bcT_at = evac_at


    y_asm_done = {}
    dts_done = {}
    scan_mm = 0
    for grp in range(4):
        grp_free = y_asm_done.get(grp - 1, 0)

        def ldh(e, grp=grp, w=grp_free):
            s_v.wait(e, w)
            for j in range(4):
                ec = grp * 4 + j
                ld_hg.dma(e, hid4[:, j * 2048:(j + 1) * 2048],
                          hidden_d[ec * 128:(ec + 1) * 128, :])
                ld_hg.dma(e, gs4[:, j * 2048:(j + 1) * 2048],
                          gs_d[ec * 128:(ec + 1) * 128, :])
        p.on("sync", ldh)
        ldh_at = ld_hg.v

        for j in range(4):
            ec = grp * 4 + j
            for tspan in range(4):
                def t_dts(e, ec=ec, tspan=tspan, w=ld0_at, wd=s_s.v,
                          wdt=evac_at, wv=s_v.v):
                    ld_once.wait(e, w)
                    s_s.wait(e, max(wd, wdt))
                    s_v.wait(e, wv)
                    e.matmul(ps_a[:, 0:512],
                             wdts[:, ec * 128:(ec + 1) * 128],
                             dtT[:, tspan * 512:(tspan + 1) * 512],
                             start=True, stop=True).then_inc(s_t.h, 1)
                s_t.v += 1
                p.on("tensor", t_dts)

                def s_dts(e, ec=ec, j=j, tspan=tspan, w=s_t.v):
                    s_t.wait(e, w)
                    sl = slice(tspan * 512, (tspan + 1) * 512)
                    e.activation(sftp[:, sl], ps_a[:, 0:512], AF.Exp,
                                 bias=bdts[:, ec:ec + 1])
                    e.activation(dts4[:, j * 2048 + tspan * 512:
                                      j * 2048 + (tspan + 1) * 512],
                                 sftp[:, sl], AF.Ln, bias=1.0)
                p.on("scalar", s_dts)
            dts_done[ec] = s_s.v

        for j in range(4):
            def v_du(e, j=j, wh=ldh_at, wd=dts_done[grp * 4 + j]):
                ld_hg.wait(e, wh)
                s_s.wait(e, wd)
                e.tensor_tensor(du4[:, j * 2048:(j + 1) * 2048],
                                dts4[:, j * 2048:(j + 1) * 2048],
                                hid4[:, j * 2048:(j + 1) * 2048], ALU.mult)
            p.on("vector", v_du)

            def v_yz(e, j=j):
                e.memset(yacc4[:, j * 2048:(j + 1) * 2048], 0.0)
            p.on("vector", v_yz)
        du_at = s_v.v

        for n in range(16):
            for which, dst in ((0, bbc), (1, cbc)):
                for sp in range(4):
                    def t_b(e, which=which, n=n, sp=sp, w=bcT_at,
                            ev=s_v.v):
                        s_s.wait(e, w)
                        s_v.wait(e, ev)
                        row = which * 16 + n
                        e.matmul(ps_bc[:, 0:512],
                                 sel32[:, row * 128:(row + 1) * 128],
                                 bcT[:, sp * 512:(sp + 1) * 512],
                                 start=True, stop=True).then_inc(s_t.h, 1)
                    s_t.v += 1
                    p.on("tensor", t_b)

                    def v_b(e, dst=dst, sp=sp, w=s_t.v):
                        s_t.wait(e, w)
                        e.tensor_copy(dst[:, sp * 512:(sp + 1) * 512],
                                      ps_bc[:, 0:512])
                    p.on("vector", v_b)
            bc_at = s_v.v

            for j in range(4):
                ec = grp * 4 + j
                slot = (n * 4 + j) % 2

                def s_da(e, ec=ec, j=j, slot=slot, wd=dts_done[ec],
                         wv=s_v.v, wa=aneg_at):
                    s_v.wait(e, max(wv, wa))
                    e.activation(dab[:, slot * 2048:(slot + 1) * 2048],
                                 dts4[:, j * 2048:(j + 1) * 2048], AF.Exp,
                                 scale=aneg[:, ec * 16 + n:ec * 16 + n + 1])
                p.on("scalar", s_da)
                da_at = s_s.v

                def v_dbu(e, j=j, slot=slot, w=bc_at, wdu=du_at):
                    s_v.wait(e, max(w, wdu))
                    e.tensor_tensor(dbub[:, slot * 2048:(slot + 1) * 2048],
                                    du4[:, j * 2048:(j + 1) * 2048],
                                    bbc[:, :], ALU.mult)
                p.on("vector", v_dbu)

                def v_scan(e, j=j, slot=slot, wda=da_at):
                    s_s.wait(e, wda)
                    e.tensor_tensor_scan(
                        states[:, :], dab[:, slot * 2048:(slot + 1) * 2048],
                        dbub[:, slot * 2048:(slot + 1) * 2048], 0.0,
                        ALU.mult, ALU.add)
                    e.tensor_tensor(cs[:, :], states[:, :], cbc[:, :],
                                    ALU.mult)
                    e.tensor_tensor(yacc4[:, j * 2048:(j + 1) * 2048],
                                    yacc4[:, j * 2048:(j + 1) * 2048],
                                    cs[:, :], ALU.add)
                p.on("vector", v_scan)

        for j in range(4):
            ec = grp * 4 + j

            def v_y(e, ec=ec, j=j, w=ld0_at, ws=st_y.v):
                ld_once.wait(e, w)
                st_y.wait(e, ws)
                e.scalar_tensor_tensor(
                    ytmp[:, :], hid4[:, j * 2048:(j + 1) * 2048],
                    ds[:, ec:ec + 1], yacc4[:, j * 2048:(j + 1) * 2048],
                    ALU.mult, ALU.add)
                e.tensor_tensor(y16[:, :], ytmp[:, :],
                                gs4[:, j * 2048:(j + 1) * 2048], ALU.mult)
            p.on("vector", v_y)

            def st_yf(e, ec=ec, w=s_v.v):
                s_v.wait(e, w)
                st_y.dma(e, y_d[ec * 128:(ec + 1) * 128, :], y16[:, :])
            p.on("gpsimd", st_yf)
        y_asm_done[grp] = s_v.v

    def fin(e):
        st_y.wait(e)
    p.on("gpsimd", fin)
    return p



def build_l4():
    p = Prog("l4")
    y_d = p.dram_in("y_d", [2048, 2048], BF16)
    wout16 = p.dram_in("wout16", [2048, 2048], BF16)
    po_d = p.dram_out("po_d", [2048, 2048], BF16)

    ysb = p.sbuf("ysb", [128, 16 * 2048], BF16)
    wsb = p.sbuf("wsb", [128, 16 * 2048], BF16)
    po16 = p.sbuf("po16", [128, 2 * 2048], BF16)

    ps0 = p.psum("ps0", [128, 2048])
    ps1 = p.psum("ps1", [128, 2048])

    s_s = p.eng_sem("scalar")
    s_t = p.sem("s_t")
    ld_once = Stream(p, "ld_once")
    st_po = [Stream(p, "st_po0"), Stream(p, "st_po1")]

    def ld0(e):
        ld_once.dma(e, ysb[:, :].rearrange("p (ec t) -> p ec t", t=2048),
                    y_d[:, :].rearrange("(ec p) t -> p ec t", p=128))
        ld_once.dma(e, wsb[:, :].rearrange("p (ec h) -> p ec h", h=2048),
                    wout16[:, :].rearrange("(ec p) h -> p ec h", p=128))
    p.on("sync", ld0)
    ld_at = ld_once.v

    ev_at = {}
    for tt in range(16):
        ps = ps0 if tt % 2 == 0 else ps1
        for ec in range(16):
            for hq in range(4):
                def tmo(e, tt=tt, ec=ec, hq=hq, ps=ps, w=ld_at,
                        ev=ev_at.get(tt - 2, 0)):
                    if ec == 0 and hq == 0:
                        ld_once.wait(e, w)
                        s_s.wait(e, ev)
                    ins = e.matmul(
                        ps[:, hq * 512:(hq + 1) * 512],
                        ysb[:, ec * 2048 + tt * 128:ec * 2048 +
                            (tt + 1) * 128],
                        wsb[:, ec * 2048 + hq * 512:ec * 2048 +
                            (hq + 1) * 512],
                        start=(ec == 0), stop=(ec == 15),
                        skip_group_check=True)
                    if ec == 15 and hq == 3:
                        ins.then_inc(s_t.h, 1)
                p.on("tensor", tmo)
        s_t.v += 1
        mm_at = s_t.v

        def s_ev(e, tt=tt, ps=ps, w=mm_at, ws=st_po[tt % 2].v):
            s_t.wait(e, w)
            st_po[tt % 2].wait(e, ws)
            for hq in range(4):
                e.activation(po16[:, (tt % 2) * 2048 + hq * 512:
                                  (tt % 2) * 2048 + (hq + 1) * 512],
                             ps[:, hq * 512:(hq + 1) * 512], AF.Copy)
        p.on("scalar", s_ev)
        ev_at[tt] = s_s.v

        def st_p(e, tt=tt, w=ev_at[tt]):
            s_s.wait(e, w)
            st_po[tt % 2].dma(e, po_d[tt * 128:(tt + 1) * 128, :],
                              po16[:, (tt % 2) * 2048:(tt % 2 + 1) * 2048])
        p.on("gpsimd", st_p)

    def fin(e):
        st_po[0].wait(e)
        st_po[1].wait(e)
    p.on("gpsimd", fin)
    return p



def build_l5():
    p = Prog("l5")
    po_a = p.dram_in("po_a", [1024, 2048], BF16)
    po_b = p.dram_in("po_b", [1024, 2048], BF16)
    x_h = p.dram_in("x_h", [1024, 2048], F32)
    mod_flat = p.dram_in("mod_flat", [4, 12288], F32)
    qg_full = p.dram_in("qg_full", [2048, I_FFN], BF16)
    qu_full = p.dram_in("qu_full", [2048, I_FFN], BF16)
    qd_full = p.dram_in("qd_full", [I_FFN, 2048], BF16)

    out_d = p.dram_out("out_d", [1024, 2048], F32)

    rows1 = p.sbuf("rows1", [1, 2048], F32)
    reps = p.sbuf("reps", [128, 4 * 2048], BF16)
    repw = p.sbuf("repw", [128, 2048], F32)
    ones1 = p.sbuf("ones1", [1, 128], F32)
    po2 = p.sbuf("po2", [128, 2 * 2048], BF16)
    xsb = p.sbuf("xsb", [128, 2048], F32)
    r2 = p.sbuf("r2", [128, 2048], F32)
    z2 = p.sbuf("z2", [128, 2048], BF16)
    h2 = p.sbuf("h2", [128, 2048], BF16)
    sums = p.sbuf("sums", [128, 8], F32)
    stat = p.sbuf("stat", [128, 8], F32)
    dump16 = p.sbuf("dump16", [128, 2048], BF16)
    qx16 = p.sbuf("qx16", [128, 2048], BF16)
    qxT = p.sbuf("qxT", [128, 16 * 128], BF16)
    qw2 = p.sbuf("qw2", [128, 4 * 512], BF16)
    sg16 = p.sbuf("sg16", [128, 512], BF16)
    tmid = p.sbuf("tmid", [128, I_FFN], BF16)
    qt16 = p.sbuf("qt16", [128, I_FFN], BF16)
    qtT = p.sbuf("qtT", [128, 44 * 128], BF16)
    qwd = p.sbuf("qwd", [128, 2 * 512], BF16)
    ffn16 = p.sbuf("ffn16", [128, 2048], BF16)
    outb = p.sbuf("outb", [128, 2048], F32)
    rowmax = p.sbuf("rowmax", [128, 2], F32)
    lnw = p.sbuf("lnw", [128, 2], F32)
    sa = p.sbuf("sa", [128, I_FFN], F32)
    sq = p.sbuf("sq", [128, I_FFN], F32)
    se = p.sbuf("se", [128, I_FFN], I16)
    qi = p.sbuf("qi", [128, I_FFN], I16)

    ps_b = p.psum("ps_b", [128, 512])
    ps_t = p.psum("ps_t", [128, 512], BF16)
    ps_g = p.psum("ps_g", [128, 512])
    ps_u = p.psum("ps_u", [128, 512])
    ps_d = p.psum("ps_d", [128, 512])

    s_v = p.eng_sem("vector")
    s_s = p.eng_sem("scalar")
    s_g = p.eng_sem("gpsimd")
    s_t = p.sem("s_t")
    ld_once = Stream(p, "ld_once")
    ld_po = [Stream(p, "ld_po0"), Stream(p, "ld_po1")]
    ld_x = [Stream(p, "ld_x0"), Stream(p, "ld_x1")]
    ld_g = [Stream(p, "ld_g0"), Stream(p, "ld_g1")]
    ld_u = [Stream(p, "ld_u0"), Stream(p, "ld_u1")]
    ld_dn = [Stream(p, "ld_dn0"), Stream(p, "ld_dn1")]
    st_o = [Stream(p, "st_o0"), Stream(p, "st_o1")]

    emit_consts(p, [1.0 / 448.0, -3.0 * LN2, 3.0 * LN2, -LN2, LN2,
                    1.0 / 2048.0, -0.5, 1.0])
    ident16, _, ident_at = emit_ident(p)

    def g_ones(e):
        e.memset(ones1[:, :], 1.0)
    p.on("gpsimd", g_ones)
    ones_at = s_g.v

    for j, rowidx in enumerate((2, 5, 4, 3)):
        def ldr(e, j=j, rowidx=rowidx, wt=s_t.v):
            e.wait_ge(s_t.h, wt)
            ld_once.dma(e, rows1[:, :],
                        mod_flat[0, rowidx * 2048:(rowidx + 1) * 2048])
        p.on("sync", ldr)
        ldr_at = ld_once.v

        for sp in range(4):
            def t_bc(e, j=j, sp=sp, w=ldr_at, wo=ones_at, ev=s_v.v):
                s_g.wait(e, wo)
                ld_once.wait(e, w)
                s_v.wait(e, ev)
                e.matmul(ps_b[:, :], ones1[:, :],
                         rows1[:, sp * 512:(sp + 1) * 512],
                         start=True, stop=True).then_inc(s_t.h, 1)
            s_t.v += 1
            p.on("tensor", t_bc)

            def v_bc(e, j=j, sp=sp, w=s_t.v):
                s_t.wait(e, w)
                e.tensor_copy(reps[:, j * 2048 + sp * 512:j * 2048 +
                                   (sp + 1) * 512], ps_b[:, :])
            p.on("vector", v_bc)
    def v_fix(e):
        for j in (0, 1):
            sl = slice(j * 2048, (j + 1) * 2048)
            e.tensor_single_scalar(repw[:, :], reps[:, sl], 0.0, ALU.is_ge)
            e.tensor_scalar(repw[:, :], repw[:, :], 2.0, -1.0, ALU.mult,
                            ALU.add)
            e.tensor_tensor(dump16[:, :], repw[:, :], reps[:, sl], ALU.mult)
            e.tensor_scalar(dump16[:, :], dump16[:, :], 0.25, None, ALU.max)
            e.tensor_tensor(reps[:, sl], repw[:, :], dump16[:, :], ALU.mult)
        e.tensor_scalar(reps[:, 2 * 2048:3 * 2048], reps[:, 2 * 2048:3 * 2048],
                        1.0, None, ALU.add)
    p.on("vector", v_fix)
    reps_at = s_v.v

    poa = p.sbuf("poa", [128, 2048], BF16)
    pob = p.sbuf("pob", [128, 2048], BF16)

    for tt in range(8):
        oslot = tt % 2

        def ldt(e, tt=tt, w=s_v.v):
            s_v.wait(e, w)
            ld_po[0].dma(e, poa[:, :], po_a[tt * 128:(tt + 1) * 128, :])
            ld_po[1].dma(e, pob[:, :], po_b[tt * 128:(tt + 1) * 128, :])
            ld_x[0].dma(e, xsb[:, :], x_h[tt * 128:(tt + 1) * 128, :])
        p.on("sync", ldt)
        w_po0, w_po1, w_x = ld_po[0].v, ld_po[1].v, ld_x[0].v

        def v_r2(e, w0=w_po0, w1=w_po1, wx=w_x, wr=reps_at):
            ld_po[0].wait(e, w0)
            ld_po[1].wait(e, w1)
            ld_x[0].wait(e, wx)
            s_v.wait(e, wr)
            e.tensor_tensor(dump16[:, :], poa[:, :], pob[:, :], ALU.add)
            e.tensor_tensor(dump16[:, :], dump16[:, :], reps[:, 0:2048],
                            ALU.mult)
            e.tensor_tensor(r2[:, :], xsb[:, :], dump16[:, :], ALU.add)
        p.on("vector", v_r2)
        r2_at = s_v.v

        def s_st2(e, w=r2_at):
            s_v.wait(e, w)
            e.activation(dump16[:, :], r2[:, :], AF.Copy,
                         accum_out=sums[:, 0:1])
            e.activation(dump16[:, :], r2[:, :], AF.Square,
                         accum_out=sums[:, 1:2])
        p.on("scalar", s_st2)
        st2_at = s_s.v

        def v_st(e, w=st2_at):
            s_s.wait(e, w)
            e.tensor_scalar(stat[:, 0:1], sums[:, 0:1], 1.0 / 2048.0, None,
                            ALU.mult)
            e.tensor_scalar(stat[:, 1:2], sums[:, 1:2], 1.0 / 2048.0, None,
                            ALU.mult)
            e.tensor_tensor(stat[:, 2:3], stat[:, 0:1], stat[:, 0:1],
                            ALU.mult)
            e.tensor_tensor(stat[:, 1:2], stat[:, 1:2], stat[:, 2:3],
                            ALU.subtract)
            e.tensor_scalar(stat[:, 1:2], stat[:, 1:2], 1e-5, None, ALU.add)
        p.on("vector", v_st)
        vst_at = s_v.v

        def s_rs(e, w=vst_at):
            s_v.wait(e, w)
            e.activation(stat[:, 1:2], stat[:, 1:2], AF.Ln)
            e.activation(stat[:, 2:3], stat[:, 1:2], AF.Exp, scale=-0.5)
        p.on("scalar", s_rs)
        rs_at = s_s.v

        def v_h2(e, w=rs_at):
            s_s.wait(e, w)
            e.tensor_scalar(z2[:, :], r2[:, :], stat[:, 0:1], stat[:, 2:3],
                            ALU.subtract, ALU.mult)
            e.tensor_tensor(h2[:, :], z2[:, :], reps[:, 2 * 2048:3 * 2048],
                            ALU.mult)
            e.tensor_tensor(h2[:, :], h2[:, :], reps[:, 3 * 2048:4 * 2048],
                            ALU.add)
        p.on("vector", v_h2)

        emit_fp8_tile(p, s_v, s_s, h2, qx16, 128, 2048, rowmax, lnw,
                      sa, sq, se, qi,
                      pre_wait=lambda e, w=s_t.v: e.wait_ge(s_t.h, w))
        qx_at = s_v.v

        for hk in range(16):
            def t_tq(e, hk=hk, w=qx_at, iw=ident_at, wt=s_t.v, ev=s_s.v):
                s_v.wait(e, max(w, iw))
                s_s.wait(e, ev)
                e.transpose(ps_t[:, 0:128],
                            qx16[:, hk * 128:(hk + 1) * 128],
                            ident16[:, :]).then_inc(s_t.h, 1)
            s_t.v += 1
            p.on("tensor", t_tq)

            def s_tq(e, hk=hk, w=s_t.v):
                s_t.wait(e, w)
                e.activation(qxT[:, hk * 128:(hk + 1) * 128], ps_t[:, 0:128],
                             AF.Copy)
            p.on("scalar", s_tq)
        qxT_at = s_s.v

        gu_mm = {}
        for cspan in range(11):
            cw = 512 if cspan < 10 else I_FFN - 10 * 512
            tmid_read_at = s_v.v
            silu_at = s_s.v
            for hk in range(16):
                wslot = hk % 2

                def ldg(e, tt=tt, cspan=cspan, hk=hk, wslot=wslot, cw=cw,
                        wt=gu_mm.get((cspan, hk - 2),
                                     gu_mm.get((cspan - 1, 14 + (hk % 2) - 0),
                                               0) if hk < 2 else 0)):
                    s_t.wait(e, wt)
                    ld_g[wslot].dma(
                        e, qw2[:, wslot * 512:wslot * 512 + cw],
                        qg_full[hk * 128:(hk + 1) * 128,
                                cspan * 512:cspan * 512 + cw])
                    ld_u[wslot].dma(
                        e, qw2[:, 1024 + wslot * 512:1024 + wslot * 512 + cw],
                        qu_full[hk * 128:(hk + 1) * 128,
                                cspan * 512:cspan * 512 + cw])
                p.on("sync", ldg)
                wg_at, wu_at = ld_g[wslot].v, ld_u[wslot].v

                def t_gu(e, hk=hk, wslot=wslot, cw=cw, wg=wg_at, wu=wu_at,
                         wq=qxT_at, wv=tmid_read_at, ws=silu_at):
                    ld_g[wslot].wait(e, wg)
                    ld_u[wslot].wait(e, wu)
                    if hk == 0:
                        s_s.wait(e, max(wq, ws))
                        s_v.wait(e, wv)
                    e.matmul(ps_g[:, 0:cw],
                             qxT[:, hk * 128:(hk + 1) * 128],
                             qw2[:, wslot * 512:wslot * 512 + cw],
                             start=(hk == 0), stop=(hk == 15),
                             skip_group_check=True).then_inc(s_t.h, 1)
                    e.matmul(ps_u[:, 0:cw],
                             qxT[:, hk * 128:(hk + 1) * 128],
                             qw2[:, 1024 + wslot * 512:
                                 1024 + wslot * 512 + cw],
                             start=(hk == 0), stop=(hk == 15),
                             skip_group_check=True).then_inc(s_t.h, 1)
                p.on("tensor", t_gu)
                s_t.v += 2
                gu_mm[(cspan, hk)] = s_t.v
            gu_at = s_t.v

            def s_silu(e, cspan=cspan, cw=cw, w=gu_at):
                s_t.wait(e, w)
                e.activation(sg16[:, 0:cw], ps_g[:, 0:cw], AF.Silu)
            p.on("scalar", s_silu)
            sg_at = s_s.v

            def v_tm(e, cspan=cspan, cw=cw, w=sg_at):
                s_s.wait(e, w)
                e.tensor_tensor(tmid[:, cspan * 512:cspan * 512 + cw],
                                sg16[:, 0:cw], ps_u[:, 0:cw], ALU.mult)
            p.on("vector", v_tm)

        emit_fp8_tile(p, s_v, s_s, tmid, qt16, 128, I_FFN, rowmax, lnw,
                      sa, sq, se, qi,
                      pre_wait=lambda e, w=s_t.v: e.wait_ge(s_t.h, w))
        qt_at = s_v.v

        for ik in range(44):
            def t_td(e, ik=ik, w=qt_at, iw=ident_at, ev=s_s.v):
                s_v.wait(e, max(w, iw))
                s_s.wait(e, ev)
                e.transpose(ps_t[:, 0:128],
                            qt16[:, ik * 128:(ik + 1) * 128],
                            ident16[:, :]).then_inc(s_t.h, 1)
            s_t.v += 1
            p.on("tensor", t_td)

            def s_td(e, ik=ik, w=s_t.v):
                s_t.wait(e, w)
                e.activation(qtT[:, ik * 128:(ik + 1) * 128], ps_t[:, 0:128],
                             AF.Copy)
            p.on("scalar", s_td)
        qtT_at = s_s.v

        for hspan in range(4):
            ffn_ev_at = s_s.v
            dn_mm = {}
            for ik in range(44):
                wslot = ik % 2

                def ldd(e, tt=tt, hspan=hspan, ik=ik, wslot=wslot,
                        wt=dn_mm.get(ik - 2, 0) if True else 0):
                    s_t.wait(e, wt)
                    ld_dn[wslot].dma(
                        e, qwd[:, wslot * 512:(wslot + 1) * 512],
                        qd_full[ik * 128:(ik + 1) * 128,
                                hspan * 512:(hspan + 1) * 512])
                p.on("sync", ldd)
                wd_at = ld_dn[wslot].v

                def t_dn(e, ik=ik, wslot=wslot, wd=wd_at, wq=qtT_at,
                         wev=ffn_ev_at):
                    ld_dn[wslot].wait(e, wd)
                    if ik == 0:
                        s_s.wait(e, max(wq, wev))
                    e.matmul(ps_d[:, :],
                             qtT[:, ik * 128:(ik + 1) * 128],
                             qwd[:, wslot * 512:(wslot + 1) * 512],
                             start=(ik == 0), stop=(ik == 43),
                             skip_group_check=True).then_inc(s_t.h, 1)
                p.on("tensor", t_dn)
                s_t.v += 1
                dn_mm[ik] = s_t.v
            dn_at = s_t.v

            def s_fev(e, hspan=hspan, w=dn_at):
                s_t.wait(e, w)
                e.activation(ffn16[:, hspan * 512:(hspan + 1) * 512],
                             ps_d[:, :], AF.Copy)
            p.on("scalar", s_fev)
        ffn_at = s_s.v

        def v_out(e, w=ffn_at, ws=st_o[0].v):
            s_s.wait(e, w)
            st_o[0].wait(e, ws)
            e.tensor_tensor(dump16[:, :], ffn16[:, :],
                            reps[:, 2048:2 * 2048], ALU.mult)
            e.tensor_tensor(outb[:, :], r2[:, :], dump16[:, :], ALU.add)
        p.on("vector", v_out)
        vout_at = s_v.v

        def st_of(e, tt=tt, w=vout_at):
            s_v.wait(e, w)
            st_o[0].dma(e, out_d[tt * 128:(tt + 1) * 128, :], outb[:, :])
        p.on("gpsimd", st_of)

    def fin(e):
        st_o[0].wait(e)
        st_o[1].wait(e)
    p.on("gpsimd", fin)
    return p

def host_prep(inputs):
    x = np.ascontiguousarray(np.asarray(inputs["L0_attn_input_states"],
                                        dtype=np.float32))
    W_in = np.asarray(inputs["L0_attn_W_in"], np.float32)
    ck = np.asarray(inputs["L0_attn_conv_kernel"], np.float32)
    cb = np.asarray(inputs["L0_attn_conv_bias"], np.float32)
    W_x = np.asarray(inputs["L0_attn_W_x"], np.float32)
    W_dt = np.asarray(inputs["L0_attn_W_dt"], np.float32)
    b_dt = np.asarray(inputs["L0_attn_b_dt"], np.float32)
    W_out = np.asarray(inputs["L0_attn_W_out"], np.float32)
    A_log = np.asarray(inputs["L0_attn_A_log"], np.float32)
    D = np.asarray(inputs["L0_attn_D"], np.float32)
    w_gate = np.asarray(inputs["L0_ffn_w_gate"], np.float32)
    w_up = np.asarray(inputs["L0_ffn_w_up"], np.float32)
    w_down = np.asarray(inputs["L0_ffn_w_down"], np.float32)
    down_idx = np.asarray(inputs["L0_ffn_down_reorder_index"], np.int64)
    c = np.asarray(inputs["L0_c"], np.float32)
    wmod = np.asarray(inputs["L0_wmod"], np.float32)

    wg_p = w_gate[down_idx]
    wu_p = w_up[down_idx]

    cores = []
    for cid in range(NCORES):
        b = cid >> 1
        eh = cid & 1
        sl = slice(eh * EH, (eh + 1) * EH)
        gsl = slice(E + eh * EH, E + (eh + 1) * EH)
        r0, r1 = GU_OFF[cid], GU_OFF[cid + 1]
        wg_sh = np.zeros((GU_PAD, 2048), np.float32)
        wu_sh = np.zeros((GU_PAD, 2048), np.float32)
        wg_sh[: r1 - r0] = wg_p[r0:r1]
        wu_sh[: r1 - r0] = wu_p[r0:r1]
        d = {
            "ct_s": np.ascontiguousarray(
                c.T[cid * 256:(cid + 1) * 256, :]),
            "wmod_s": np.ascontiguousarray(wmod[cid * 256:(cid + 1) * 256]),
            "w_in_s": np.ascontiguousarray(
                np.concatenate([W_in[:, sl], W_in[:, gsl]], axis=1)),
            "wout_s": np.ascontiguousarray(W_out[sl]),
            "wx_s": np.ascontiguousarray(W_x[sl]),
            "wdt_s": np.ascontiguousarray(W_dt[:, sl]),
            "ck_r": np.ascontiguousarray(
                ck[sl, 0, :].reshape(16, 128, 4).transpose(1, 0, 2)
                .reshape(128, 64)),
            "wg_s": wg_sh,
            "wu_s": wu_sh,
            "wd_s": np.ascontiguousarray(w_down[cid * 256:(cid + 1) * 256]),
            "x_b": x[b],
            "cb_r": np.ascontiguousarray(cb[sl].reshape(16, 128).T),
            "bdt_r": np.ascontiguousarray(b_dt[sl].reshape(16, 128).T),
            "alog_r": np.ascontiguousarray(
                A_log[sl].reshape(16, 128, 16).transpose(1, 0, 2)
                .reshape(128, 256)),
            "d_r": np.ascontiguousarray(D[sl].reshape(16, 128).T),
        }
        cores.append(d)
    return cores


L1_INPUTS = ["ct_s", "wmod_s", "w_in_s", "wout_s", "wx_s", "wdt_s",
             "ck_r", "wg_s", "wu_s", "wd_s"]


def run_l1(cores):
    p1 = build_l1()
    in_maps = [{k: cores[cid][k] for k in L1_INPUTS} for cid in range(NCORES)]
    return run_bass_kernel_spmd(p1.nc, in_maps,
                                core_ids=list(range(NCORES))).results


def run_l2(cores, r1):
    p2 = build_l2()
    mod_parts = [np.asarray(r1[j]["mod_part"]) for j in range(NCORES)]
    in_maps = []
    for cid in range(NCORES):
        b = cid >> 1
        rot = [(b + k) % 4 for k in range(4)]
        mod_all = np.stack([mp[rot] for mp in mod_parts])
        in_maps.append({
            "x_b": cores[cid]["x_b"],
            "mod_all": mod_all,
            "w_in16": np.asarray(r1[cid]["w_in16"]),
            "diag_d": np.asarray(r1[cid]["diag_d"]),
            "cb_r": cores[cid]["cb_r"],
            "wx16": np.asarray(r1[cid]["wx16"]),
        })
    return run_bass_kernel_spmd(p2.nc, in_maps,
                                core_ids=list(range(NCORES))).results


def run_l3(cores, r1, r2):
    p3 = build_l3()
    in_maps = []
    for cid in range(NCORES):
        in_maps.append({
            "ssmp_a": np.asarray(r2[cid]["ssmp_part"]),
            "ssmp_b": np.asarray(r2[cid ^ 1]["ssmp_part"]),
            "wdt16": np.asarray(r1[cid]["wdt16"]),
            "bdt_r": cores[cid]["bdt_r"],
            "alog_r": cores[cid]["alog_r"],
            "d_r": cores[cid]["d_r"],
            "hidden_d": np.asarray(r2[cid]["hidden_d"]),
            "gs_d": np.asarray(r2[cid]["gs_d"]),
        })
    return run_bass_kernel_spmd(p3.nc, in_maps,
                                core_ids=list(range(NCORES))).results


def run_l4(cores, r1, r3):
    p4 = build_l4()
    in_maps = [{"y_d": np.asarray(r3[cid]["y_d"]),
                "wout16": np.asarray(r1[cid]["wout16"])}
               for cid in range(NCORES)]
    return run_bass_kernel_spmd(p4.nc, in_maps,
                                core_ids=list(range(NCORES))).results


def run_l5(cores, r1, r2, r4):
    p5 = build_l5()
    qg_full = np.concatenate(
        [np.asarray(r1[j]["qg_sh"])[:, :GU_TILES[j] * 128]
         for j in range(NCORES)], axis=1)
    qu_full = np.concatenate(
        [np.asarray(r1[j]["qu_sh"])[:, :GU_TILES[j] * 128]
         for j in range(NCORES)], axis=1)
    qd_full = np.concatenate(
        [np.asarray(r1[j]["qd_sh"]) for j in range(NCORES)], axis=1)
    in_maps = []
    for cid in range(NCORES):
        eh = cid & 1
        rs = slice(eh * 1024, (eh + 1) * 1024)
        in_maps.append({
            "po_a": np.asarray(r4[cid]["po_d"])[rs],
            "po_b": np.asarray(r4[cid ^ 1]["po_d"])[rs],
            "x_h": np.ascontiguousarray(cores[cid]["x_b"][rs]),
            "mod_flat": np.asarray(r2[cid]["mod_flat"]),
            "qg_full": qg_full,
            "qu_full": qu_full,
            "qd_full": qd_full,
        })
    return run_bass_kernel_spmd(p5.nc, in_maps,
                                core_ids=list(range(NCORES))).results


def kernel(**inputs):
    cores = host_prep(inputs)
    r1 = run_l1(cores)
    r2 = run_l2(cores, r1)
    r3 = run_l3(cores, r1, r2)
    r4 = run_l4(cores, r1, r3)
    r5 = run_l5(cores, r1, r2, r4)
    out = np.empty((B, L, H), np.float32)
    for cid in range(NCORES):
        b = cid >> 1
        eh = cid & 1
        out[b, eh * 1024:(eh + 1) * 1024] = np.asarray(r5[cid]["out_d"])
    return out
